# version 01
# baseline (speedup 1.0000x reference)
import math
import numpy as np
import concourse.bass as bass
import concourse.mybir as mybir
from concourse.bass_utils import run_bass_kernel_spmd

F32 = mybir.dt.float32
BF16 = mybir.dt.bfloat16
AF = mybir.ActivationFunctionType
ALU = mybir.AluOpType
AX = mybir.AxisListType.X

D = 1024
NDB = 8
H = 16
DH = 64
HP = 8
ROW = 4096
WIN = 256
DI = 2730
DI2 = 5460
GT = 3
BT = 32
EA = 29
SUB = [(27, 'a'), (27, 'f'), (25, 'a'), (25, 'f'), (23, 'a'), (23, 'f'),
       (21, 'a'), (21, 'f'), (19, 'a'), (19, 'f'), (17, 'a'), (16, 'f')]
OUTT = 16
PW = 5
NFT = 22
FW = [128] * 21 + [DI - 21 * 128]
NCORES = 8

ENGS = ('tensor', 'vector', 'scalar', 'gpsimd', 'sync')


class Prog:

    NLANE = 8
    STRICT = frozenset({'vector', 'scalar', 'gpsimd', 'sync'})
    TLO = -1
    THI = -1

    def __init__(self, nc, sems):
        self.nc = nc
        self.sems = sems
        self.q = {e: [] for e in ENGS}
        self.cnt = {k: 0 for k in sems}
        self.water = {e: {k: 0 for k in sems} for e in ENGS}
        self.ops_since_inc = {e: 0 for e in ENGS}
        self.last_tok = {e: None for e in ENGS}
        self.eng_sem = {e: e for e in ENGS}
        self.lane_rr = {}

    def emit(self, eng, fn, waits=(), inc=None, tag=""):
        real_waits = []
        st = Prog.STRICT
        self.eidx = getattr(self, 'eidx', 0) + 1
        strict_here = (st is True or (st and eng in st))
        if eng == 'tensor' and not strict_here and Prog.TLO <= self.eidx < Prog.THI:
            strict_here = True
        if strict_here:
            ke = self.eng_sem[eng]
            if self.cnt[ke] > self.water[eng][ke]:
                self.water[eng][ke] = self.cnt[ke]
                real_waits.append((ke, self.cnt[ke]))
            if inc is None:
                inc = 'eng'
        for (s, v) in waits:
            if v <= 0:
                continue
            if self.eng_sem.get(eng) == s:
                continue
            if self.water[eng][s] >= v:
                continue
            self.water[eng][s] = v
            real_waits.append((s, v))
        tok = None
        key = None
        if inc == 'eng':
            key = self.eng_sem[eng]
            self.cnt[key] += 1
            tok = (key, self.cnt[key])
        elif inc is not None:
            lane = self.lane_rr.get(inc, 0)
            self.lane_rr[inc] = (lane + 1) % self.NLANE
            key = f"{inc}{lane}"
            if self.cnt[key] > 0 and self.water[eng][key] < self.cnt[key]:
                self.water[eng][key] = self.cnt[key]
                real_waits.append((key, self.cnt[key]))
            self.cnt[key] += 16
            tok = (key, self.cnt[key])
        self.q[eng].append((fn, real_waits, key, tag))
        self.ops_since_inc[eng] = 0 if key is not None else self.ops_since_inc[eng] + 1
        if key is not None:
            self.last_tok[eng] = tok
        return tok

    def t(self, fn, waits=(), inc=None, tag=""):
        return self.emit('tensor', fn, waits, inc, tag)

    def v(self, fn, waits=(), inc=None, tag=""):
        return self.emit('vector', fn, waits, inc, tag)

    def s(self, fn, waits=(), inc=None, tag=""):
        return self.emit('scalar', fn, waits, inc, tag)

    def g(self, fn, waits=(), inc=None, tag=""):
        return self.emit('gpsimd', fn, waits, inc, tag)

    def y(self, fn, waits=(), inc=None, tag=""):
        return self.emit('sync', fn, waits, inc, tag)

    def barrier(self):
        toks = []
        for e in ENGS:
            if self.ops_since_inc[e] > 0:
                self.q[e].append(('MARK_INC', None, self.eng_sem[e], ''))
                self.cnt[self.eng_sem[e]] += 1
                self.ops_since_inc[e] = 0
                self.last_tok[e] = (self.eng_sem[e], self.cnt[self.eng_sem[e]])
            if self.last_tok[e] is not None:
                toks.append(self.last_tok[e])
        for k in self.sems:
            if k not in ENGS and self.cnt[k] > 0:
                toks.append((k, self.cnt[k]))
        for e in ENGS:
            for (s, v) in toks:
                if self.eng_sem.get(e) == s:
                    continue
                if self.water[e][s] < v:
                    self.q[e].append(('WAIT', (s, v), None, ''))
                    self.water[e][s] = v
        return toks

    def flush(self, block):
        if not hasattr(self, 'cursor'):
            self.cursor = {e: 0 for e in ENGS}
            self.last_inst = {e: None for e in ENGS}
        for e in ENGS:
            pend = self.q[e][self.cursor[e]:]
            self.cursor[e] = len(self.q[e])
            if not pend:
                continue

            def make_runner(eng_name, items):
                def _run(h):
                    for (fn, waits, inc_key, tag) in items:
                        if fn == 'MARK_INC':
                            li_ = self.last_inst[eng_name]
                            assert li_ is not None, f"MARK_INC with no prior op on {eng_name}"
                            li_.then_inc(self.sems[inc_key], 1)
                            continue
                        if fn == 'WAIT':
                            sn, v = waits
                            h.wait_ge(self.sems[sn], v)
                            continue
                        for (sn, v) in waits:
                            h.wait_ge(self.sems[sn], v)
                        inst = fn(h)
                        if inc_key is not None:
                            inst.then_inc(self.sems[inc_key], 1 if inc_key in ENGS else 16)
                        if inst is not None:
                            self.last_inst[eng_name] = inst
                return _run

            getattr(block, e)(make_runner(e, pend))


def _rotary_tables():
    t = np.arange(ROW, dtype=np.float64)
    half = np.arange(0, DH, 2, dtype=np.float64)
    inv_freq = 1.0 / np.power(10000.0, half / DH)
    freqs = t[:, None] * inv_freq[None, :]
    freqs = np.concatenate([freqs, freqs], axis=1)
    base = (half + 0.4 * DH) / (1.4 * DH)
    power = (t - ROW // 2) / (WIN // 2)
    scale = np.exp(power[:, None] * np.log(base)[None, :])
    scale = np.concatenate([scale, scale], axis=1)
    cos = np.cos(freqs)
    sin = np.sin(freqs)
    sinsig = np.concatenate([-sin[:, :32], sin[:, 32:]], axis=1)
    return cos, sinsig, scale, 1.0 / scale


def _tables_for_core(tok_of_col):
    cos, sinsig, scale, inv = _ROT
    tk = np.clip(tok_of_col, 0, ROW - 1)
    Ak = (cos[tk] * inv[tk]).T.astype(np.float32)
    Bk = (sinsig[tk] * inv[tk]).T.astype(np.float32)
    ktab = np.concatenate([Ak, Bk], axis=0)
    nq = BT - GT
    qtab = np.zeros((128, nq * 128), dtype=np.float32)
    for j in range(nq):
        cols = tk[(GT + j) * 128:(GT + j + 1) * 128]
        qtab[:, j * 128:j * 128 + 64] = (cos[cols] * scale[cols]).astype(np.float32)
        qtab[:, j * 128 + 64:(j + 1) * 128] = (sinsig[cols] * scale[cols]).astype(np.float32)
    return np.ascontiguousarray(ktab), np.ascontiguousarray(qtab)


_ROT = _rotary_tables()


def _masks_for_core(is_left):
    import ml_dtypes
    j = np.arange(128)[:, None]
    i = np.arange(128)[None, :]
    far = (j >= i).astype(np.float32)
    ones = np.ones((128, 128), np.float32)
    if is_left:
        m = np.zeros((3, 128, 128), np.float32)
    else:
        m = np.stack([far, ones, far])
    return m.astype(ml_dtypes.bfloat16)


def _const_masks():
    import ml_dtypes
    j = np.arange(128)[:, None]
    i = np.arange(128)[None, :]
    diag = (j <= i).astype(np.float32)
    far = (j >= i).astype(np.float32)
    return diag.astype(ml_dtypes.bfloat16), far.astype(ml_dtypes.bfloat16)


def _swap_perm():
    import ml_dtypes
    P = np.zeros((128, 128), np.float32)
    for m in range(128):
        base = (m // 64) * 64
        r = m % 64
        c = base + (r + 32) % 64
        P[c, m] = 1.0
    return P.astype(ml_dtypes.bfloat16)


def build_nc(n_sub=12, dbg_out=None, dumps=False):
    import contextlib
    nc = bass.Bass("TRN2", target_bir_lowering=False)

    def din(name, shape, dt=F32):
        return nc.dram_tensor(name, shape, dt, kind="ExternalInput").ap()

    xkv = din("xkv", [ROW, D])
    ecol = din("ecol", [128, 4])
    xq = din("xq", [EA * 128, D])
    zc8 = din("zc8", [128, 8])
    wq = din("wq", [D, D]); wk = din("wk", [D, D])
    wv = din("wv", [D, D]); wo = din("wo", [D, D])
    bocol = din("bocol", [128, 8])
    qsma = din("qsma", [128, D]); ksma = din("ksma", [128, D])
    kscola = din("kscola", [128, 8])
    wqkv = din("wqkv", [6, D, 3 * D])
    wo6 = din("wo6", [6, D, D])
    w1 = din("w1", [6, D, DI2])
    w2 = din("w2", [6, DI, D])
    b1col = din("b1col", [6, 128, 44])
    b2col = din("b2col", [6, 128, 8])
    lngcol = din("lngcol", [6, 128, 8]); lnbcol = din("lnbcol", [6, 128, 8])
    qsm6 = din("qsm6", [6, 128, DH]); qsm6s = din("qsm6s", [6, 128, DH])
    ksm6 = din("ksm6", [6, 128, DH]); kscol6 = din("kscol6", [6, 128, 1])
    ktabAi = din("ktabA", [128, BT * 128]); ktabBi = din("ktabB", [128, BT * 128])
    qtabi = din("qtab", [128, (BT - GT) * 128])
    pmaski = din("pmask", [3, 128, 128], BF16)
    mdiagi = din("mdiag", [128, 128], BF16); mfari = din("mfar", [128, 128], BF16)
    ident32 = din("ident32", [128, 128]); identb = din("identb", [128, 128], BF16)
    pibi = din("pib", [128, 128], BF16)
    onescol = din("onescol", [128, 1], BF16)
    onesblk = din("onesblk", [128, 2], BF16)
    onesrow = din("onesrow", [1, 128], BF16)

    out = nc.dram_tensor("out", [OUTT * 128, D], F32, kind="ExternalOutput").ap()
    ikind = dict(kind="ExternalOutput") if dumps else {}
    aodbg = nc.dram_tensor("aodbg", [128, NDB * 512 * 8], BF16,
                           kind="ExternalOutput").ap() if dumps else None
    yh = nc.dram_tensor("yh", [NDB * 128, BT * 128], F32, **ikind).ap()
    xah = nc.dram_tensor("xah", [NDB * 128, BT * 128], F32, **ikind).ap()
    qah = nc.dram_tensor("qah", [NDB * 128, EA * 128], BF16, **ikind).ap()
    vah = nc.dram_tensor("vah", [HP * 128, BT * 128], BF16, **ikind).ap()

    stack = contextlib.ExitStack()
    sems = {}
    for name in list(ENGS):
        sems[name] = stack.enter_context(nc.semaphore(name))
    for qn in ('dgp', 'dsy'):
        for ln in range(Prog.NLANE):
            sems[f"{qn}{ln}"] = stack.enter_context(nc.semaphore(f"{qn}{ln}"))
    p = Prog(nc, sems)
    DG, DS = 'dgp', 'dsy'
    block = stack.enter_context(nc.Block())
    sb = lambda name, shape, dt: stack.enter_context(nc.sbuf_tensor(name, shape, dt))

    PS = stack.enter_context(nc.psum_tensor("PS", [128, 4096], F32))

    def BANK(i, n=1, bf=False):
        ap = PS[:, i * 512:(i + n) * 512]
        return ap.bitcast(BF16) if bf else ap

    c_id32 = sb("c_id32", [128, 128], F32)
    c_idb = sb("c_idb", [128, 128], BF16)
    c_ocol = sb("c_ocol", [128, 1], BF16)
    c_oblk = sb("c_oblk", [128, 2], BF16)
    c_orow = sb("c_orow", [1, 128], BF16)
    s_ecol = sb("s_ecol", [128, 4], F32)
    s_row = sb("s_row", [1, 512], F32)
    s_rhi = sb("s_rhi", [1, 512], BF16)
    s_rlo32 = sb("s_rlo32", [1, 512], F32)
    s_rlo = sb("s_rlo", [1, 512], BF16)
    s_mu = sb("s_mu", [1, 512], F32)
    s_var = sb("s_var", [1, 512], F32)
    s_rst = sb("s_rst", [1, 512], F32)
    s_mr = sb("s_mr", [1, 512], F32)
    SH = {}
    s_sqb = sb("s_sqb", [128, 512], BF16)
    s_lnt16 = sb("s_lnt16", [128, 16], F32)
    s_qf = sb("s_qf", [128, D], F32)
    s_qsqb = sb("s_qsqb", [128, D], BF16)
    s_ssq = sb("s_ssq", [128, H], F32)
    s_rstq = sb("s_rstq", [128, H], F32)
    s_ub = sb("s_ub", [128, D], BF16)
    s_qhb = sb("s_qhb", [128, D], BF16)
    tok_c = []
    for dst, src in [(s_ecol, ecol), (c_id32, ident32), (c_idb, identb),
                     (c_ocol, onescol), (c_oblk, onesblk), (c_orow, onesrow)]:
        tok_c.append(p.y(lambda h, d=dst, s_=src: h.dma_start(out=d[:, :], in_=s_), inc=DS))
    tok_consts = list(tok_c)

    def attn_unit(hp, kb_list, qrhs, kfn, vfn, rsqfn, negc_ap,
                  psA, psB, po0, po1, pd0, pd1, first_tok_waits):
        pbh = SH['pbh']
        n = len(kb_list)
        exp_tok = {}
        pv_tok = {}
        last_dn = None
        for i in range(n + 1):
            if i < n:
                kb, qoff, kw, masks = kb_list[i]
                par = i % 2
                wts = list(first_tok_waits) if i == 0 else []
                if i >= 2:
                    wts += [exp_tok[(i - 2, 0)], exp_tok[(i - 2, 1)]]
                kt = kfn(kb)
                p.t(lambda h_, kt=kt, par=par, qoff=qoff, kw=kw: h_.matmul(
                    psA[:, par * 512:par * 512 + kw], kt[0:64, :], qrhs(qoff, kw, 0),
                    start=True, stop=True, tile_position=(0, 0)), waits=wts)
                tq = p.t(lambda h_, kt=kt, par=par, qoff=qoff, kw=kw: h_.matmul(
                    psB[:, par * 512:par * 512 + kw], kt[64:128, :], qrhs(qoff, kw, 1),
                    start=True, stop=True, tile_position=(64, 0)), inc='eng')
                for h in range(2):
                    ps = psA if h == 0 else psB
                    wts2 = [tq]
                    if i >= 2:
                        wts2.append(pv_tok[(i - 2, h)])
                    sc_ap = rsqfn(kb, h)
                    bi_ap = negc_ap(h)
                    et = p.s(lambda h_, h=h, par=par, kw=kw, ps=ps, sc_ap=sc_ap,
                             bi_ap=bi_ap: h_.activation(
                        pbh[(h, par)][:, 0:kw], ps[:, par * 512:par * 512 + kw], AF.Exp,
                        bias=bi_ap, scale=sc_ap), waits=wts2, inc='eng')
                    for (moff, mask_ap) in masks:
                        et = p.v(lambda h_, h=h, par=par, moff=moff, mask_ap=mask_ap:
                                 h_.tensor_tensor(
                                     pbh[(h, par)][:, moff:moff + 128],
                                     pbh[(h, par)][:, moff:moff + 128],
                                     mask_ap, ALU.mult), waits=[et], inc='eng')
                    exp_tok[(i, h)] = et
            if i >= 1:
                kb, qoff, kw, masks = kb_list[i - 1]
                par = (i - 1) % 2
                st = (i == 1)
                sp = (i == n)
                for h in range(2):
                    vt = vfn(kb, 2 * hp + h)
                    pvp = po0 if h == 0 else po1
                    pdp = pd0 if h == 0 else pd1
                    p.t(lambda h_, vt=vt, h=h, par=par, qoff=qoff, kw=kw, st=st, sp=sp,
                        pvp=pvp: h_.matmul(
                        pvp[64 * h:64 * h + 64, qoff:qoff + kw], vt, pbh[(h, par)][:, 0:kw],
                        start=st, stop=sp, tile_position=(0, 64 * h)),
                        waits=[exp_tok[(i - 1, h)]])
                    td = p.t(lambda h_, h=h, par=par, qoff=qoff, kw=kw, st=st, sp=sp,
                             pdp=pdp: h_.matmul(
                        pdp[0:1, qoff:qoff + kw], c_ocol[:, :], pbh[(h, par)][:, 0:kw],
                        start=st, stop=sp), inc='eng')
                    pv_tok[(i - 1, h)] = td
                    if sp and h == 1:
                        last_dn = td
        return last_dn

    def attn_norm(hp, w, po0, po1, pd0, pd1, pbc, tok_dn, fr):
        s_aof = SH['aof']
        s_aost = SH['aost']
        toks = []
        for h in range(2):
            pdp = pd0 if h == 0 else pd1
            wl = [tok_dn] + ([fr['row']] if fr.get('row') else [])
            p.v(lambda h_, pdp=pdp, w=w: h_.reciprocal(s_row[:, 0:w], pdp[0:1, 0:w]),
                waits=wl)
            p.v(lambda h_, w=w: h_.tensor_copy(s_rhi[:, 0:w], s_row[:, 0:w]))
            p.v(lambda h_, w=w: h_.tensor_tensor(s_rlo32[:, 0:w], s_row[:, 0:w],
                                                 s_rhi[:, 0:w], ALU.subtract))
            cl = p.v(lambda h_, w=w: h_.tensor_copy(s_rlo[:, 0:w], s_rlo32[:, 0:w]), inc='eng')
            wpb = [cl] + ([fr['pbc']] if (h == 0 and fr.get('pbc')) else [])
            p.t(lambda h_, h=h, w=w, pbc=pbc: h_.matmul(
                pbc[64 * h:64 * h + 64, 0:w], c_orow[:, 0:64], s_rhi[:, 0:w],
                start=True, stop=False, tile_position=(0, 64 * h)), waits=wpb)
            tb = p.t(lambda h_, h=h, w=w, pbc=pbc: h_.matmul(
                pbc[64 * h:64 * h + 64, 0:w], c_orow[:, 0:64], s_rlo[:, 0:w],
                start=False, stop=True, tile_position=(0, 64 * h)), inc='eng')
            toks.append(tb)
            fr['row'] = tb
        wa = [tok_dn] + ([fr['aof']] if fr.get('aof') else [])
        p.s(lambda h_, w=w, po0=po0: h_.copy(s_aof[0:64, 0:w], po0[0:64, 0:w]), waits=wa)
        e1 = p.s(lambda h_, w=w, po1=po1: h_.copy(s_aof[64:128, 0:w], po1[64:128, 0:w]),
                 inc='eng')
        mt = p.v(lambda h_, hp=hp, w=w, pbc=pbc: h_.tensor_tensor(
            s_aost[:, hp * 512:hp * 512 + w], s_aof[:, 0:w], pbc[:, 0:w], ALU.mult),
            waits=[e1] + toks, inc='eng')
        fr['aof'] = mt
        fr['pbc'] = mt
        return mt

    astk1 = contextlib.ExitStack()
    astk2 = contextlib.ExitStack()
    asb1 = lambda name, shape, dt: astk1.enter_context(nc.sbuf_tensor(name, shape, dt))
    asb2 = lambda name, shape, dt: astk2.enter_context(nc.sbuf_tensor(name, shape, dt))
    s_kA = asb2("s_kA", [128, HP * ROW], BF16)
    s_rsqA = asb2("s_rsqA", [128, BT * 16], F32)
    s_woT = asb2("s_woT", [128, NDB * D], BF16)
    s_negcA = asb2("s_negcA", [128, 16], F32)
    s_boc = asb2("s_boc", [128, 8], F32)
    s_wqT = asb1("s_wqT", [128, NDB * D], BF16)
    s_wkT = asb1("s_wkT", [128, NDB * D], BF16)
    s_wvT = asb1("s_wvT", [128, NDB * D], BF16)
    s_qsma = asb1("s_qsma", [128, D], F32)
    s_kcolA = asb1("s_kcolA", [128, 8], F32)
    s_vstg = asb1("s_vstg", [128, 8 * 512], BF16)

    for dst, src in [(s_qsma, qsma), (s_kcolA, kscola), (s_boc, bocol)]:
        tok_c.append(p.y(lambda h, d=dst, s_=src: h.dma_start(out=d[:, :], in_=s_), inc=DS))
    tks = p.y(lambda h: h.dma_start(out=s_qf[:, :], in_=ksma), inc=DS)
    p.v(lambda h: h.tensor_tensor(s_qf[:, :], s_qsma[:, :], s_qf[:, :], ALU.mult),
        waits=[tok_c[-3], tks])
    p.v(lambda h: h.tensor_reduce(
        s_negcA[:, :], s_qf[:, :].rearrange("p (h d) -> p h d", h=H), AX, ALU.max,
        apply_absolute_value=True))
    p.v(lambda h: h.tensor_scalar(s_negcA[:, :], s_negcA[:, :], 0.125, -44.0, ALU.mult, ALU.add))
    p.v(lambda h: h.tensor_scalar_max(s_negcA[:, :], s_negcA[:, :], 0.0))
    tok_negA = p.v(lambda h: h.tensor_scalar_mul(s_negcA[:, :], s_negcA[:, :], -1.0), inc='eng')

    wstk = contextlib.ExitStack()
    if True:
        wprep = wstk.enter_context(nc.sbuf_tensor("wprep", [128, D], BF16))
        pwt = BANK(0, 1, bf=True)
        prev_ev = None
        for wsrc, wdstT in [(wq, s_wqT), (wk, s_wkT), (wv, s_wvT), (wo, s_woT)]:
            for rt in range(8):
                wl = [prev_ev] if prev_ev else []
                tl = p.g(lambda h, ws=wsrc, rt=rt: h.dma_start(
                    out=wprep[:, :], in_=ws[rt * 128:(rt + 1) * 128, :]), inc=DG, waits=wl)
                tr = None
                for ct in range(8):
                    tr = p.t(lambda h, ct=ct: h.transpose(
                        pwt[:, ct * 128:(ct + 1) * 128], wprep[:, ct * 128:(ct + 1) * 128],
                        c_idb[:, :]), waits=[tl] + tok_consts if ct == 0 else [],
                        inc='eng' if ct == 7 else None)
                prev_ev = p.v(lambda h, wd=wdstT, rt=rt: h.tensor_copy(
                    wd[:, :].rearrange("p (ct f) -> p ct f", ct=8)[:, :, rt * 128:rt * 128 + 128],
                    pwt[:, :].rearrange("p (ct f) -> p ct f", ct=8)), waits=[tr], inc='eng')
    if dumps:
        wdbg = nc.dram_tensor("wdbg", [128, NDB * D], BF16, kind="ExternalOutput").ap()
        p.barrier()
        p.y(lambda h: h.dma_start(out=wdbg, in_=s_wkT[:, :]), inc=DS)
    p.barrier()
    p.flush(block)
    wstk.close()

    stk1 = contextlib.ExitStack()
    if True:
        sb1 = lambda n_, sh_, dt_: stk1.enter_context(nc.sbuf_tensor(n_, sh_, dt_))
        xld = sb1("xld", [128, 2 * D], F32)
        lnbt = sb1("lnbt", [128, 2 * D], BF16)
        lnxT = sb1("lnxT", [128, NDB * 512], BF16)
        bst = sb1("bst", [128, 12], F32)
        bmv = sb1("bmv", [128, 2], F32)
        r2c = sb1("r2c", [128, 1], F32)
        nmr = sb1("nmr", [128, 1], F32)
        qtmp = sb1("qtmp", [128, NDB * 512], BF16)

        ptr = BANK(0, 1, bf=True)
        pa0 = BANK(1)
        pa1 = BANK(2)
        pq = BANK(3, 2)
        pv = BANK(5, 2)
        prs = BANK(7)

        xld_free = [None, None]
        lnbt_free = [None, None]
        lnxT_free = [None, None]
        ptr_free = None
        qtmp_free = None
        vstg_free = None
        prs_free = None
        sqb_free = None
        qsq_free = None
        qf_free = None
        qhb_free = None
        bmv_free = None
        r2c_free = None
        nmr_free = None
        pa_free = [None, None]
        pq_free = None
        pv_free = None

        def ln_tile(src, tt, par, cpar, pos):
            nonlocal bmv_free, r2c_free, nmr_free, ptr_free
            wl = [t for t in (xld_free[par],) if t]
            tld = p.y(lambda h, tt=tt, par=par, src=src: h.dma_start(
                out=xld[:, par * D:(par + 1) * D], in_=src[tt * 128:(tt + 1) * 128, :]),
                waits=wl, inc=DS)
            p.v(lambda h, par=par: h.bn_stats(bst[:, 0:6], xld[:, par * D:par * D + 512]),
                waits=[tld] + ([bmv_free] if bmv_free else []))
            p.v(lambda h, par=par: h.bn_stats(bst[:, 6:12],
                                              xld[:, par * D + 512:(par + 1) * D]))
            tbn = p.v(lambda h: h.bn_aggr(bmv[:, :], bst[:, :]), inc='eng')
            p.s(lambda h: h.activation(r2c[:, :], bmv[:, 1:2], AF.Ln, bias=s_ecol[:, 0:1]),
                waits=[tbn] + ([r2c_free] if r2c_free else []))
            trs = p.s(lambda h: h.activation(r2c[:, :], r2c[:, :], AF.Exp, scale=-0.5),
                      inc='eng')
            bmv_free = trs
            tnm = p.v(lambda h: h.scalar_tensor_tensor(
                nmr[:, :], bmv[:, 0:1], -1.0, r2c[:, :], ALU.mult, ALU.mult),
                waits=[trs] + ([nmr_free] if nmr_free else []), inc='eng')
            tln = p.s(lambda h, par=par: h.activation(
                lnbt[:, par * D:(par + 1) * D], xld[:, par * D:(par + 1) * D],
                AF.Identity, bias=nmr[:, :], scale=r2c[:, :]),
                waits=[tnm] + ([lnbt_free[par]] if lnbt_free[par] else []), inc='eng')
            xld_free[par] = tln
            r2c_free = tln
            nmr_free = tln
            wtr = [tln] + ([ptr_free] if ptr_free else [])
            ttr = None
            for k in range(8):
                ttr = p.t(lambda h, k=k, par=par: h.transpose(
                    ptr[:, k * 128:(k + 1) * 128],
                    lnbt[:, par * D + k * 128:par * D + (k + 1) * 128],
                    c_idb[:, :]), waits=wtr if k == 0 else [],
                    inc='eng' if k == 7 else None)
            lnbt_free[par] = ttr
            wev = [ttr] + ([lnxT_free[cpar]] if (pos == 0 and lnxT_free[cpar]) else [])
            ptr_free = p.v(lambda h, pos=pos: h.tensor_copy(
                lnxT[:, :].rearrange("p (db f) -> p db f", db=NDB)[:, :, pos * 128:(pos + 1) * 128],
                ptr[:, :].rearrange("p (db f) -> p db f", db=NDB)), waits=wev, inc='eng')
            return ptr_free

        for c in range(8):
            cpar = 0
            for pos in range(4):
                tok_lnxT = ln_tile(xkv, 4 * c + pos, (4 * c + pos) % 2, cpar, pos)
            lx = lnxT[:, :]
            for et in range(8):
                ppar = et % 2
                pa = pa0 if ppar == 0 else pa1
                wmm = [tok_lnxT] + ([pa_free[ppar]] if pa_free[ppar] else [])
                tm = None
                for db in range(NDB):
                    tm = p.t(lambda h, et=et, db=db, pa=pa: h.matmul(
                        pa[:, :], s_wkT[:, db * D + et * 128:db * D + (et + 1) * 128],
                        lx[:, db * 512:(db + 1) * 512],
                        start=(db == 0), stop=(db == NDB - 1)),
                        waits=wmm if db == 0 else [], inc='eng' if db == NDB - 1 else None)
                tsq = p.s(lambda h, pa=pa: h.activation(s_sqb[:, :], pa[:, :], AF.Square),
                          waits=[tm] + ([sqb_free] if sqb_free else []), inc='eng')
                wpr = [tsq] + ([prs_free] if (et == 0 and prs_free) else [])
                tmini = None
                for tb in range(4):
                    tmini = p.t(lambda h, tb=tb, et=et: h.matmul(
                        prs[:, tb * 16 + 2 * et:tb * 16 + 2 * et + 2],
                        s_sqb[:, tb * 128:(tb + 1) * 128], c_oblk[:, :],
                        start=True, stop=True),
                        waits=wpr if tb == 0 else [], inc='eng' if tb == 3 else None)
                sqb_free = tmini
                pa_free[ppar] = p.s(lambda h, et=et, c=c, pa=pa: h.activation(
                    s_kA[:, et * ROW + c * 512:et * ROW + c * 512 + 512], pa[:, :],
                    AF.Copy, scale=s_kcolA[:, et:et + 1]), waits=[tm], inc='eng')
            last_mini = sqb_free
            for tb in range(4):
                p.s(lambda h, tb=tb: h.activation(
                    s_lnt16[:, :], prs[:, tb * 16:(tb + 1) * 16], AF.Ln, bias=s_ecol[:, 1:2]),
                    waits=[last_mini])
                prs_free = p.s(lambda h, tb=tb, c=c: h.activation(
                    s_rsqA[:, (4 * c + tb) * 16:(4 * c + tb + 1) * 16], s_lnt16[:, :],
                    AF.Exp, scale=-0.5), inc='eng')
            tvm = None
            for tb in range(4):
                tt = 4 * c + tb
                wv0 = [tok_lnxT] + ([pv_free] if pv_free else [])
                for half in range(2):
                    for db in range(NDB):
                        tvm = p.t(lambda h, half=half, db=db, tb=tb: h.matmul(
                            pv[:, half * 512:(half + 1) * 512],
                            lx[:, db * 512 + tb * 128:db * 512 + (tb + 1) * 128],
                            s_wvT[:, db * D + half * 512:db * D + (half + 1) * 512],
                            start=(db == 0), stop=(db == NDB - 1)),
                            waits=wv0 if (half == 0 and db == 0) else [],
                            inc='eng' if (half == 1 and db == NDB - 1) else None)
                wvs = [tvm] + (vstg_free if (tb == 0 and vstg_free) else [])
                pv_free = p.s(lambda h, tb=tb: h.copy(
                    s_vstg[:, :].rearrange("p (hp f) -> p hp f", hp=HP)[:, :, tb * 128:(tb + 1) * 128],
                    pv[:, :].rearrange("p (hp f) -> p hp f", hp=HP)), waits=wvs, inc='eng')
            lnxT_free[cpar] = tvm
            dvt = []
            for hp_ in range(HP):
                dvt.append(p.y(lambda h, hp_=hp_, c=c: h.dma_start(
                    out=vah[hp_ * 128:(hp_ + 1) * 128, c * 512:(c + 1) * 512],
                    in_=s_vstg[:, hp_ * 512:(hp_ + 1) * 512]),
                    waits=[pv_free] if hp_ == 0 else [], inc=DS))
            vstg_free = dvt

        for c in range(8):
            cpar = 0
            qts = [t for t in range(4 * c, 4 * c + 4) if t < EA]
            for pos, tt in enumerate(qts):
                tok_lnxT = ln_tile(xq, tt, tt % 2, cpar, pos)
            lx = lnxT[:, :]
            tqm = None
            for tb, tt in enumerate(qts):
                wq0 = [tok_lnxT] + ([pq_free] if pq_free else [])
                for half in range(2):
                    for db in range(NDB):
                        tqm = p.t(lambda h, half=half, db=db, tb=tb: h.matmul(
                            pq[:, half * 512:(half + 1) * 512],
                            lx[:, db * 512 + tb * 128:db * 512 + (tb + 1) * 128],
                            s_wqT[:, db * D + half * 512:db * D + (half + 1) * 512],
                            start=(db == 0), stop=(db == NDB - 1)),
                            waits=wq0 if (half == 0 and db == 0) else [],
                            inc='eng' if (half == 1 and db == NDB - 1) else None)
                tqs = p.s(lambda h: h.activation(s_qsqb[:, :], pq[:, :], AF.Square),
                          waits=[tqm] + ([qsq_free] if qsq_free else []), inc='eng')
                pq_free = p.s(lambda h: h.copy(s_qf[:, :], pq[:, :]),
                              waits=([qf_free] if qf_free else []), inc='eng')
                trd = p.v(lambda h: h.tensor_reduce(
                    s_ssq[:, :], s_qsqb[:, :].rearrange("p (h d) -> p h d", h=H), AX, ALU.add),
                    waits=[tqs], inc='eng')
                qsq_free = trd
                p.s(lambda h: h.activation(s_rstq[:, :], s_ssq[:, :], AF.Ln, scale=64.0,
                                           bias=s_ecol[:, 2:3]), waits=[trd])
                tre = p.s(lambda h: h.activation(s_rstq[:, :], s_rstq[:, :], AF.Exp,
                                                 scale=-0.5), inc='eng')
                tub = p.v(lambda h: h.tensor_tensor(
                    s_ub[:, :].rearrange("p (h d) -> p h d", h=H),
                    s_qf[:, :].rearrange("p (h d) -> p h d", h=H),
                    s_rstq[:, :].unsqueeze(2).broadcast_to([128, H, DH]), ALU.mult),
                    waits=[tre], inc='eng')
                qf_free = tub
                tqh = p.v(lambda h: h.tensor_tensor(s_qhb[:, :], s_ub[:, :], s_qsma[:, :],
                                                    ALU.mult),
                          waits=([qhb_free] if qhb_free else []), inc='eng')
                wtr = [tqh] + ([ptr_free] if ptr_free else [])
                ttr = None
                for k in range(8):
                    ttr = p.t(lambda h, k=k: h.transpose(
                        ptr[:, k * 128:(k + 1) * 128], s_qhb[:, k * 128:(k + 1) * 128],
                        c_idb[:, :]), waits=wtr if k == 0 else [],
                        inc='eng' if k == 7 else None)
                qhb_free = ttr
                wev = [ttr] + (qtmp_free if (tb == 0 and qtmp_free) else [])
                ptr_free = p.v(lambda h, tb=tb: h.tensor_copy(
                    qtmp[:, :].rearrange("p (db f) -> p db f", db=NDB)[:, :, tb * 128:(tb + 1) * 128],
                    ptr[:, :].rearrange("p (db f) -> p db f", db=NDB)), waits=wev, inc='eng')
            lnxT_free[cpar] = tqm
            qw = 128 * len(qts)
            dqt = []
            for db in range(NDB):
                dqt.append(p.y(lambda h, db=db, c=c, qw=qw: h.dma_start(
                    out=qah[db * 128:(db + 1) * 128, c * 512:c * 512 + qw],
                    in_=qtmp[:, db * 512:db * 512 + qw]),
                    waits=[ptr_free] if db == 0 else [], inc=DS))
            qtmp_free = dqt

    if dumps:
        kdbg = nc.dram_tensor("kdbg", [128, HP * ROW], BF16, kind="ExternalOutput").ap()
        rdbg = nc.dram_tensor("rdbg", [128, BT * 16], F32, kind="ExternalOutput").ap()
        p.barrier()
        p.y(lambda h: h.dma_start(out=kdbg, in_=s_kA[:, :]), inc=DS)
        p.y(lambda h: h.dma_start(out=rdbg, in_=s_rsqA[:, :]), inc=DS)
    p.barrier()
    p.flush(block)
    stk1.close()
    astk1.close()

    stk2 = contextlib.ExitStack()
    if True:
        sb2 = lambda n_, sh_, dt_: stk2.enter_context(nc.sbuf_tensor(n_, sh_, dt_))
        qch = sb2("qch", [128, NDB * 512], BF16)
        vbuf = sb2("vbuf", [128, 2 * BT * 128], BF16)
        SH['pbh'] = {(hh, pp): sb2(f"pbA{hh}{pp}", [128, 512], BF16)
                     for hh in range(2) for pp in range(2)}
        SH['aof'] = sb2("aofA", [128, 512], F32)
        SH['aost'] = sb2("aostA", [128, NDB * 512], BF16)
        aostA = SH['aost']
        xrl = sb2("xrl", [128, D], F32)
        xTs = sb2("xTs", [128, NDB * 512], F32)
        xas = sb2("xas", [128, NDB * 512], F32)
        xab = sb2("xab", [128, 512], BF16)
        xsqb = sb2("xsqb", [128, 512], BF16)
        ytmp = sb2("ytmp", [128, 512], F32)
        y0sb = sb2("y0sb", [128, 512], F32)

        psA = BANK(0, 2)
        psB = BANK(2, 2)
        po0 = BANK(4)
        po1 = BANK(5)
        pd0 = BANK(6)
        pd1 = BANK(7)

        fr = {}
        unit_done = None
        qch_free = None
        xTs_free = None
        xrl_free = None
        xas_free = None
        xab_free = None
        y0_free = None
        row2_free = None
        vb_free = [None, None]
        vb_tok = [None, None]
        vb_tok[0] = p.y(lambda h: h.dma_start(out=vbuf[:, 0:BT * 128],
                                              in_=vah[0:128, :]), inc=DS)
        for qc in range(8):
            w = 512 if qc < 7 else 128
            qtoks = []
            for db in range(NDB):
                qtoks.append(p.y(lambda h, db=db, qc=qc, w=w: h.dma_start(
                    out=qch[:, db * 512:db * 512 + w],
                    in_=qah[db * 128:(db + 1) * 128, qc * 512:qc * 512 + w]),
                    waits=([qch_free] if (db == 0 and qch_free) else []), inc=DS))
            mt_toks = []
            tok_dn = None
            for hp in range(HP):
                par = hp % 2
                nhp = hp + 1 if hp < HP - 1 else 0
                npar = nhp % 2
                if not (qc == 7 and hp == HP - 1):
                    wv_ = [vb_free[npar]] if vb_free[npar] else []
                    vb_tok[npar] = p.y(lambda h, nhp=nhp, npar=npar: h.dma_start(
                        out=vbuf[:, npar * BT * 128:(npar + 1) * BT * 128],
                        in_=vah[nhp * 128:(nhp + 1) * 128, :]), waits=wv_, inc=DS)

                def qrhs(qoff, kw, h_, hp=hp, w=w):
                    return qch[64 * h_:64 * h_ + 64, hp * 512 + qoff:hp * 512 + qoff + kw]

                kb_list = [(kb, 0, w, ()) for kb in range(BT)]
                fw = list(qtoks) + [vb_tok[par]] + ([unit_done] if unit_done else [])
                tok_dn = attn_unit(
                    hp, kb_list, qrhs,
                    kfn=lambda kb, hp=hp: s_kA[:, hp * ROW + kb * 128:hp * ROW + (kb + 1) * 128],
                    vfn=lambda kb, g, par=par, hp=hp: vbuf[:, par * BT * 128 + kb * 128 +
                                                           (g - 2 * hp) * 64:
                                                           par * BT * 128 + kb * 128 +
                                                           (g - 2 * hp) * 64 + 64],
                    rsqfn=lambda kb, h_, hp=hp: s_rsqA[:, kb * 16 + 2 * hp + h_:
                                                       kb * 16 + 2 * hp + h_ + 1],
                    negc_ap=lambda h_, hp=hp: s_negcA[:, 2 * hp + h_:2 * hp + h_ + 1],
                    psA=psA, psB=psB, po0=po0, po1=po1, pd0=pd0, pd1=pd1,
                    first_tok_waits=fw)
                vb_free[par] = tok_dn
                mt = attn_norm(hp, w, po0, po1, pd0, pd1, psA, tok_dn, fr)
                mt_toks.append(mt)
                unit_done = mt
            if dumps:
                p.y(lambda h, qc=qc, w=w: h.dma_start(
                    out=aodbg[:, qc * NDB * 512:qc * NDB * 512 + NDB * 512],
                    in_=SH['aost'][:, :]), waits=[unit_done], inc=DS)
            qch_free = tok_dn
            ntb = w // 128
            xtt = None
            for tb in range(ntb):
                wlx = []
                if tb == 0 and xTs_free:
                    wlx.append(xTs_free)
                if xrl_free:
                    wlx.append(xrl_free)
                tlx = p.y(lambda h, qc=qc, tb=tb: h.dma_start(
                    out=xrl[:, :], in_=xq[(qc * 4 + tb) * 128:(qc * 4 + tb + 1) * 128, :]),
                    waits=wlx, inc=DS)
                ttr = None
                for k in range(8):
                    ttr = p.t(lambda h, k=k: h.transpose(
                        psB[:, k * 128:(k + 1) * 128], xrl[:, k * 128:(k + 1) * 128],
                        c_id32[:, :]),
                        waits=([tlx, unit_done] + ([xtt] if xtt else [])) if k == 0 else [],
                        inc='eng' if k == 7 else None)
                xrl_free = ttr
                xtt = p.v(lambda h, tb=tb: h.tensor_copy(
                    xTs[:, :].rearrange("p (db f) -> p db f", db=NDB)[:, :, tb * 128:(tb + 1) * 128],
                    psB[:, :].rearrange("p (db f) -> p db f", db=NDB)),
                    waits=[ttr], inc='eng')
            st_toks = []
            tchain = None
            tm = None
            for mt_i in range(NDB):
                pwo = po0 if mt_i % 2 == 0 else po1
                wmm = [mt_toks[-1], xtt] + (xas_free if (mt_i == 0 and xas_free) else [])
                if mt_i >= 2:
                    wmm = [st_toks[mt_i - 2]]
                for eb in range(NDB):
                    tm = p.t(lambda h, mt_i=mt_i, eb=eb, w=w, pwo=pwo: h.matmul(
                        pwo[:, 0:w], s_woT[:, eb * D + mt_i * 128:eb * D + (mt_i + 1) * 128],
                        aostA[:, eb * 512:eb * 512 + w],
                        start=(eb == 0), stop=(eb == NDB - 1)),
                        waits=wmm if eb == 0 else [], inc='eng' if eb == NDB - 1 else None)
                tst = p.v(lambda h, mt_i=mt_i, w=w, pwo=pwo: h.scalar_tensor_tensor(
                    xas[:, mt_i * 512:mt_i * 512 + w], pwo[:, 0:w], s_boc[:, mt_i:mt_i + 1],
                    xTs[:, mt_i * 512:mt_i * 512 + w], ALU.add, ALU.add),
                    waits=[tm], inc='eng')
                tcb = p.v(lambda h, mt_i=mt_i, w=w: h.tensor_copy(
                    xab[:, 0:w], xas[:, mt_i * 512:mt_i * 512 + w]),
                    waits=[tst] + ([xab_free] if xab_free else []), inc='eng')
                tsq = p.s(lambda h, w=w: h.activation(xsqb[:, 0:w], xab[:, 0:w], AF.Square),
                          waits=[tcb], inc='eng')
                p.t(lambda h, mt_i=mt_i, w=w: h.matmul(
                    pd0[0:1, 0:w], c_ocol[:, :], xab[:, 0:w],
                    start=(mt_i == 0), stop=(mt_i == NDB - 1)), waits=[tcb])
                tchain = p.t(lambda h, mt_i=mt_i, w=w: h.matmul(
                    pd1[0:1, 0:w], c_ocol[:, :], xsqb[:, 0:w],
                    start=(mt_i == 0), stop=(mt_i == NDB - 1)), waits=[tsq], inc='eng')
                xab_free = tchain
                st_toks.append(tst)
            wrow = [tchain] + ([row2_free] if row2_free else []) + \
                   ([fr['row']] if fr.get('row') else [])
            p.v(lambda h, w=w: h.tensor_scalar_mul(s_mu[:, 0:w], pd0[0:1, 0:w], 1.0 / D),
                waits=wrow)
            p.v(lambda h, w=w: h.tensor_scalar_mul(s_var[:, 0:w], pd1[0:1, 0:w], 1.0 / D))
            p.v(lambda h, w=w: h.tensor_tensor(s_mr[:, 0:w], s_mu[:, 0:w], s_mu[:, 0:w],
                                               ALU.mult))
            tvr = p.v(lambda h, w=w: h.tensor_tensor(s_var[:, 0:w], s_var[:, 0:w], s_mr[:, 0:w],
                                                     ALU.subtract), inc='eng')
            p.s(lambda h, w=w: h.activation(s_rst[:, 0:w], s_var[:, 0:w], AF.Ln,
                                            bias=s_ecol[0:1, 0:1]), waits=[tvr])
            trs2 = p.s(lambda h, w=w: h.activation(s_rst[:, 0:w], s_rst[:, 0:w], AF.Exp,
                                                   scale=-0.5), inc='eng')
            tmr = p.v(lambda h, w=w: h.tensor_tensor(s_mr[:, 0:w], s_mu[:, 0:w], s_rst[:, 0:w],
                                                     ALU.mult), waits=[trs2], inc='eng')
            bc_toks = []
            for (rowap, pbank, tokw) in ((s_rst, po0, trs2), (s_mr, po1, tmr)):
                wcp = [tokw] + ([fr['row']] if fr.get('row') else [])
                p.v(lambda h, rowap=rowap, w=w: h.tensor_copy(s_rhi[:, 0:w], rowap[:, 0:w]),
                    waits=wcp)
                p.v(lambda h, rowap=rowap, w=w: h.tensor_tensor(
                    s_rlo32[:, 0:w], rowap[:, 0:w], s_rhi[:, 0:w], ALU.subtract))
                cl = p.v(lambda h, w=w: h.tensor_copy(s_rlo[:, 0:w], s_rlo32[:, 0:w]), inc='eng')
                for hh in range(2):
                    p.t(lambda h, hh=hh, pbank=pbank, w=w: h.matmul(
                        pbank[64 * hh:64 * hh + 64, 0:w], c_orow[:, 0:64], s_rhi[:, 0:w],
                        start=True, stop=False, tile_position=(0, 64 * hh)), waits=[cl])
                    tb2 = p.t(lambda h, hh=hh, pbank=pbank, w=w: h.matmul(
                        pbank[64 * hh:64 * hh + 64, 0:w], c_orow[:, 0:64], s_rlo[:, 0:w],
                        start=False, stop=True, tile_position=(0, 64 * hh)), inc='eng')
                bc_toks.append(tb2)
                fr['row'] = tb2
            row2_free = bc_toks[-1]
            t2 = None
            xah_toks = []
            for db in range(NDB):
                t1 = p.v(lambda h, db=db, w=w: h.tensor_tensor(
                    ytmp[:, 0:w], xas[:, db * 512:db * 512 + w], po0[:, 0:w], ALU.mult),
                    waits=bc_toks, inc='eng')
                t2 = p.v(lambda h, db=db, w=w: h.tensor_tensor(
                    y0sb[:, 0:w], ytmp[:, 0:w], po1[:, 0:w], ALU.subtract),
                    waits=([y0_free] if y0_free else []), inc='eng')
                y0_free = p.y(lambda h, db=db, qc=qc, w=w: h.dma_start(
                    out=yh[db * 128:(db + 1) * 128, GT * 128 + qc * 512:GT * 128 + qc * 512 + w],
                    in_=y0sb[:, 0:w]), waits=[t2], inc=DS)
                xah_toks.append(p.y(lambda h, db=db, qc=qc, w=w: h.dma_start(
                    out=xah[db * 128:(db + 1) * 128, GT * 128 + qc * 512:GT * 128 + qc * 512 + w],
                    in_=xas[:, db * 512:db * 512 + w]), inc=DS))
            xas_free = xah_toks
            xTs_free = st_toks[-1]
            unit_done = t2
            fr['pbc'] = t2

    zz = stk2.enter_context(nc.sbuf_tensor("zz", [128, GT * 128], F32))
    tz = p.v(lambda h: h.memset(zz[:, :], 0.0), inc='eng')
    for db in range(NDB):
        p.y(lambda h, db=db: h.dma_start(
            out=yh[db * 128:(db + 1) * 128, 0:GT * 128], in_=zz[:, :]),
            waits=[tz] if db == 0 else [], inc=DS)
    p.barrier()
    p.flush(block)
    stk2.close()
    astk2.close()

    bstk = contextlib.ExitStack()
    bsb = lambda name, shape, dt: bstk.enter_context(nc.sbuf_tensor(name, shape, dt))
    KT = GT + PW
    c_pib = bsb("c_pib", [128, 128], BF16)
    c_mdiag = bsb("c_mdiag", [128, 128], BF16)
    c_mfar = bsb("c_mfar", [128, 128], BF16)
    c_pmask = bsb("c_pmask", [128, 3 * 128], BF16)
    s_t1b = bsb("s_t1b", [128, D], F32)
    s_t2b = bsb("s_t2b", [128, D], F32)
    s_qsm = bsb("s_qsm", [128, DH], F32)
    s_qsms = bsb("s_qsms", [128, DH], F32)
    s_ksm = bsb("s_ksm", [128, DH], F32)
    s_kcol = bsb("s_kcol", [128, 1], F32)
    s_negc = bsb("s_negc", [128, 1], F32)
    s_yold = bsb("s_yold", [128, 2 * 512], F32)
    s_ynew = bsb("s_ynew", [128, 2 * 512], F32)
    s_qsa64 = bsb("s_qsa64", [128, DH], BF16)
    s_qsb64 = bsb("s_qsb64", [128, DH], BF16)

    tok_tabs = []
    for dst, src in [(c_pib, pibi), (c_mdiag, mdiagi), (c_mfar, mfari)]:
        tok_tabs.append(p.y(lambda h, d=dst, s_=src: h.dma_start(out=d[:, :], in_=s_),
                            inc=DS))
    tok_tabs.append(p.y(lambda h: h.dma_start(
        out=c_pmask[:, :].rearrange("p (m f) -> p m f", m=3),
        in_=pmaski.rearrange("m p f -> p m f")), inc=DS))

    def maskfor(qb, kb, qE):
        pm = lambda i: c_pmask[:, i * 128:(i + 1) * 128]
        d_ = qb - kb
        if d_ == 0:
            return c_mdiag[:, :]
        if kb == -2 or (qb == qE and d_ == 2):
            return pm(0)
        if (kb == -1 and qb == 0) or (qb == qE and d_ == 1):
            return pm(1)
        if (kb == -1 and qb == 1) or (qb == qE + 1 and d_ == 2):
            return pm(2)
        if d_ == 2:
            return c_mfar[:, :]
        return None

    def attn_sublayer(li, E):
        bcol0 = (BT - E) * 128
        lstk = contextlib.ExitStack()
        lsb = lambda n_, sh_, dt_: lstk.enter_context(nc.sbuf_tensor(f"{n_}_{li}", sh_, dt_))
        s_ktabA = lsb("ktabA", [128, KT * 128], F32)
        s_ktabB = lsb("ktabB", [128, KT * 128], F32)
        s_qtab = lsb("qtab", [128, PW * 128], F32)
        SH['pbh'] = {(hh, pp): lsb(f"pbB{hh}{pp}", [128, 512], BF16)
                     for hh in range(2) for pp in range(2)}
        SH['aof'] = lsb("aofB", [128, 512], F32)
        SH['aost'] = lsb("aostB", [128, NDB * 512], BF16)
        aostB = SH['aost']
        s_wq6 = lsb("wq6", [128, NDB * D], BF16)
        s_wk6 = lsb("wk6", [128, NDB * D], BF16)
        s_wv6 = lsb("wv6", [128, NDB * D], BF16)
        s_wo6 = lsb("wo6", [128, NDB * D], BF16)
        s_xn = lsb("xn", [128, NDB * KT * 128], BF16)
        s_kB = lsb("kB", [128, HP * KT * 128], BF16)
        s_vB = lsb("vB", [128, KT * 1024], BF16)
        s_rsqB = lsb("rsqB", [128, KT * 16], F32)
        s_qbuf = lsb("qbuf", [128, NDB * 512], BF16)
        s_ktmp = lsb("ktmp", [128, 512], BF16)
        s_kt1 = lsb("kt1", [128, 512], BF16)
        s_kt2 = lsb("kt2", [128, 512], BF16)
        qtab_tok = [None]
        ktab_tok = [None]
        qtab_rd = [None]
        bptr = BANK(0, 1, bf=True)
        bpq = BANK(2, 2)
        bpa0 = BANK(4)
        bpa1 = BANK(5)
        bprs = BANK(6)
        bpsw = BANK(7)
        psA = BANK(0, 2)
        psB = BANK(2, 2)
        po0 = BANK(4)
        po1 = BANK(5)
        pd0 = BANK(6)
        pd1 = BANK(7)

        pl = []
        for dst, src in [(s_qsm, qsm6), (s_qsms, qsm6s), (s_ksm, ksm6), (s_kcol, kscol6)]:
            pl.append(p.y(lambda h, d=dst, s_=src, li=li: h.dma_start(
                out=d[:, :], in_=s_[li]), inc=DS))
        for db in range(NDB):
            for dst, c0, c1 in [(s_wq6, 0, D), (s_wk6, D, 2 * D), (s_wv6, 2 * D, 3 * D)]:
                pl.append(p.g(lambda h, db=db, li=li, d=dst, c0=c0, c1=c1: h.dma_start(
                    out=d[:, db * D:(db + 1) * D],
                    in_=wqkv[li, db * 128:(db + 1) * 128, c0:c1]), inc=DG))
            pl.append(p.g(lambda h, db=db, li=li: h.dma_start(
                out=s_wo6[:, db * D:(db + 1) * D],
                in_=wo6[li, db * 128:(db + 1) * 128, :]), inc=DG))
        p.v(lambda h: h.tensor_tensor(s_t1b[:, 0:DH], s_qsm[:, :], s_ksm[:, :], ALU.mult),
            waits=pl[:4])
        p.v(lambda h: h.tensor_reduce(s_negc[:, :], s_t1b[:, 0:DH], AX, ALU.max,
                                      apply_absolute_value=True))
        p.v(lambda h: h.tensor_scalar(s_negc[:, :], s_negc[:, :], 8.0, -44.0,
                                      ALU.mult, ALU.add))
        p.v(lambda h: h.tensor_scalar_max(s_negc[:, :], s_negc[:, :], 0.0))
        tok_negc = p.v(lambda h: h.tensor_scalar_mul(s_negc[:, :], s_negc[:, :], -1.0),
                       inc='eng')

        fr = {}
        unit_done = tok_negc
        xn_free = None
        kbuf_free = None
        qb_free = None
        yold_free = None
        ynew_free = [None, None]
        ktmp_free = []
        psw_free = None
        bpq_free = None
        rsq_free = None
        sqb_free = None
        passes = []
        qt0 = 0
        while qt0 < E:
            qnt = min(PW, E - qt0)
            passes.append((qt0, qnt))
            qt0 += qnt
        xn_tok = {}

        def emit_xn(pi):
            (q0_, qnt_) = passes[pi]
            if pi == 0:
                t0, nt, d0 = -GT, GT + qnt_, 0
            else:
                t0, nt, d0 = q0_, qnt_, GT
            toks = []
            for db in range(NDB):
                sh = 1 if db >= 4 else 0
                toks.append(p.g(lambda h, db=db, t0=t0, nt=nt, d0=d0, sh=sh: h.dma_start(
                    out=s_xn[:, (db * KT + d0) * 128:(db * KT + d0 + nt) * 128],
                    in_=yh[db * 128:(db + 1) * 128,
                           bcol0 + t0 * 128 - sh:bcol0 + (t0 + nt) * 128 - sh]),
                    waits=([xn_free] if (db == 0 and xn_free) else []), inc=DG))
            xn_tok[pi] = toks
            return toks

        emit_xn(0)
        for pi, (qt0, qnt) in enumerate(passes):
            kt0 = qt0 - GT
            qtw = ([qtab_tok[0]] if qtab_tok[0] else []) + \
                  ([qtab_rd[0]] if qtab_rd[0] else [])
            qtab_tok[0] = p.y(lambda h, qt0=qt0, qnt=qnt, E=E: h.dma_start(
                out=s_qtab[:, 0:qnt * 128],
                in_=qtabi[:, (BT - GT - E + qt0) * 128:(BT - GT - E + qt0 + qnt) * 128]),
                waits=qtw, inc=DS)
            ktw = (ktab_tok[0] if ktab_tok[0] else []) + ([kbuf_free] if kbuf_free else [])
            knt_ = min(KT, GT + E - (qt0 - 0))
            tkA = p.y(lambda h, kt0=kt0, knt_=knt_: h.dma_start(
                out=s_ktabA[:, 0:knt_ * 128],
                in_=ktabAi[:, bcol0 + kt0 * 128:bcol0 + (kt0 + knt_) * 128]),
                waits=ktw, inc=DS)
            tkB = p.y(lambda h, kt0=kt0, knt_=knt_: h.dma_start(
                out=s_ktabB[:, 0:knt_ * 128],
                in_=ktabBi[:, bcol0 + kt0 * 128:bcol0 + (kt0 + knt_) * 128]), inc=DS)
            ktab_tok[0] = [tkA, tkB]
            cf = []
            if pi > 0:
                pq_ = passes[pi - 1][1]
                for hp_ in range(HP):
                    cf.append(p.v(lambda h, hp_=hp_, pq_=pq_: h.tensor_copy(
                        s_kB[:, hp_ * KT * 128:(hp_ * KT + GT) * 128],
                        s_kB[:, (hp_ * KT + pq_) * 128:(hp_ * KT + pq_ + GT) * 128]),
                        waits=[unit_done] if hp_ == 0 else [], inc='eng'))
                cf.append(p.v(lambda h, pq_=pq_: h.tensor_copy(
                    s_vB[:, 0:GT * 1024], s_vB[:, pq_ * 1024:(pq_ + GT) * 1024]), inc='eng'))
                cf.append(p.v(lambda h, pq_=pq_: h.tensor_copy(
                    s_rsqB[:, 0:GT * 16], s_rsqB[:, pq_ * 16:(pq_ + GT) * 16]), inc='eng'))
            d0 = 0 if pi == 0 else GT
            nnt = (GT + qnt) if pi == 0 else qnt
            ch0 = 0
            while ch0 < nnt:
                cw_t = min(4, nnt - ch0)
                cw = cw_t * 128
                xoff = (d0 + ch0) * 128
                wx = xn_tok[pi] + cf + pl + [unit_done] + ktab_tok[0] + tok_tabs
                for et in range(8):
                    ppar = et % 2
                    pa = bpa0 if ppar == 0 else bpa1
                    tm = None
                    for db in range(NDB):
                        tm = p.t(lambda h, et=et, db=db, pa=pa, xoff=xoff, cw=cw: h.matmul(
                            pa[:, 0:cw], s_wk6[:, db * D + et * 128:db * D + (et + 1) * 128],
                            s_xn[:, db * KT * 128 + xoff:db * KT * 128 + xoff + cw],
                            start=(db == 0), stop=(db == NDB - 1)),
                            waits=wx if db == 0 else [], inc='eng' if db == NDB - 1 else None)
                    tsq = p.s(lambda h, pa=pa, cw=cw: h.activation(
                        s_sqb[:, 0:cw], pa[:, 0:cw], AF.Square),
                        waits=[tm] + ([sqb_free] if sqb_free else []), inc='eng')
                    wpr = [tsq] + ([rsq_free] if (et == 0 and ch0 == 0 and rsq_free) else [])
                    tmini = None
                    for tb in range(cw_t):
                        tmini = p.t(lambda h, tb=tb, et=et, d0=d0, ch0=ch0: h.matmul(
                            bprs[:, (d0 + ch0 + tb) * 16 + 2 * et:(d0 + ch0 + tb) * 16 + 2 * et + 2],
                            s_sqb[:, tb * 128:(tb + 1) * 128], c_oblk[:, :],
                            start=True, stop=True),
                            waits=wpr if tb == 0 else [], inc='eng' if tb == cw_t - 1 else None)
                    sqb_free = tmini
                    tkk = p.s(lambda h, pa=pa, cw=cw: h.activation(
                        s_ktmp[:, 0:cw], pa[:, 0:cw], AF.Copy, scale=s_kcol[:, :]),
                        waits=[tm] + ktmp_free, inc='eng')
                    tsw = p.t(lambda h, cw=cw: h.matmul(
                        bpsw[:, 0:cw], c_pib[:, :], s_ktmp[:, 0:cw], start=True, stop=True),
                        waits=[tkk] + ([psw_free] if psw_free else []), inc='eng')
                    ta1 = p.v(lambda h, xoff=xoff, cw=cw: h.tensor_tensor(
                        s_kt1[:, 0:cw], s_ktmp[:, 0:cw],
                        s_ktabA[:, xoff:xoff + cw], ALU.mult), waits=[tkk], inc='eng')
                    ta2 = p.v(lambda h, xoff=xoff, cw=cw: h.tensor_tensor(
                        s_kt2[:, 0:cw], bpsw[:, 0:cw],
                        s_ktabB[:, xoff:xoff + cw], ALU.mult), waits=[tsw], inc='eng')
                    psw_free = ta2
                    ktmp_free = [ta1, tsw]
                    tkf = p.v(lambda h, et=et, xoff=xoff, cw=cw: h.tensor_tensor(
                        s_kB[:, et * KT * 128 + xoff:et * KT * 128 + xoff + cw],
                        s_kt1[:, 0:cw], s_kt2[:, 0:cw], ALU.add), inc='eng')
                    kbuf_free = tkf
                for tb in range(cw_t):
                    pt = d0 + ch0 + tb
                    p.s(lambda h, pt=pt: h.activation(
                        s_lnt16[:, :], bprs[:, pt * 16:(pt + 1) * 16], AF.Ln, bias=s_ecol[:, 1:2]),
                        waits=[sqb_free])
                    rsq_free = p.s(lambda h, pt=pt: h.activation(
                        s_rsqB[:, pt * 16:(pt + 1) * 16], s_lnt16[:, :], AF.Exp,
                        scale=-0.5), inc='eng')
                for tb in range(cw_t):
                    pt = d0 + ch0 + tb
                    tvm = None
                    wv0 = [bpq_free] if bpq_free else []
                    for half in range(2):
                        for db in range(NDB):
                            tvm = p.t(lambda h, half=half, db=db, pt=pt: h.matmul(
                                bpq[:, half * 512:(half + 1) * 512],
                                s_xn[:, db * KT * 128 + pt * 128:db * KT * 128 + (pt + 1) * 128],
                                s_wv6[:, db * D + half * 512:db * D + (half + 1) * 512],
                                start=(db == 0), stop=(db == NDB - 1)),
                                waits=wv0 if (half == 0 and db == 0) else [],
                                inc='eng' if (half == 1 and db == NDB - 1) else None)
                    bpq_free = p.s(lambda h, pt=pt: h.copy(
                        s_vB[:, pt * 1024:(pt + 1) * 1024], bpq[:, :]), waits=[tvm], inc='eng')
                ch0 += cw_t
            nqc = (qnt + 3) // 4
            for j in range(nqc):
                q0 = qt0 + 4 * j
                wq_t = min(4, qt0 + qnt - q0)
                w = wq_t * 128
                tq_toks = []
                for tb in range(wq_t):
                    qt = q0 + tb
                    pl_t = qt - kt0
                    tqm = None
                    wq0 = [unit_done, bpq_free] if tb == 0 else []
                    for half in range(2):
                        for db in range(NDB):
                            tqm = p.t(lambda h, half=half, db=db, pl_t=pl_t: h.matmul(
                                bpq[:, half * 512:(half + 1) * 512],
                                s_xn[:, db * KT * 128 + pl_t * 128:db * KT * 128 + (pl_t + 1) * 128],
                                s_wq6[:, db * D + half * 512:db * D + (half + 1) * 512],
                                start=(db == 0), stop=(db == NDB - 1)),
                                waits=wq0 if (half == 0 and db == 0) else [],
                                inc='eng' if (half == 1 and db == NDB - 1) else None)
                    tqs = p.s(lambda h: h.activation(s_qsqb[:, :], bpq[:, :], AF.Square),
                              waits=[tqm], inc='eng')
                    tqf = p.s(lambda h: h.copy(s_qf[:, :], bpq[:, :]), inc='eng')
                    bpq_free = tqf
                    trd = p.v(lambda h: h.tensor_reduce(
                        s_ssq[:, :], s_qsqb[:, :].rearrange("p (h d) -> p h d", h=H),
                        AX, ALU.add), waits=[tqs], inc='eng')
                    p.s(lambda h: h.activation(s_rstq[:, :], s_ssq[:, :], AF.Ln,
                                               scale=1.0 / 64.0, bias=s_ecol[:, 3:4]), waits=[trd])
                    tre = p.s(lambda h: h.activation(s_rstq[:, :], s_rstq[:, :], AF.Exp,
                                                     scale=-0.5), inc='eng')
                    tub = p.v(lambda h: h.tensor_tensor(
                        s_ub[:, :].rearrange("p (h d) -> p h d", h=H),
                        s_qf[:, :].rearrange("p (h d) -> p h d", h=H),
                        s_rstq[:, :].unsqueeze(2).broadcast_to([128, H, DH]), ALU.mult),
                        waits=[tre], inc='eng')
                    qtoff = (qt - qt0) * 128
                    p.v(lambda h, qtoff=qtoff: h.tensor_tensor(
                        s_qsa64[:, :], s_qsm[:, :], s_qtab[:, qtoff:qtoff + DH], ALU.mult),
                        waits=[qtab_tok[0]])
                    tqs2 = p.v(lambda h, qtoff=qtoff: h.tensor_tensor(
                        s_qsb64[:, :], s_qsms[:, :], s_qtab[:, qtoff + DH:qtoff + 128],
                        ALU.mult), inc='eng')
                    qtab_rd[0] = tqs2
                    tt1 = p.v(lambda h: h.tensor_tensor(
                        s_t1b[:, :].rearrange("p (h d) -> p h d", h=H),
                        s_ub[:, :].rearrange("p (h d) -> p h d", h=H),
                        s_qsa64[:, :].unsqueeze(1).broadcast_to([128, H, DH]), ALU.mult),
                        waits=[tub, tqs2], inc='eng')
                    p.v(lambda h: h.tensor_tensor(
                        s_t2b[:, :].rearrange("p (h d) -> p h d", h=H)[:, :, 0:32],
                        s_ub[:, :].rearrange("p (h d) -> p h d", h=H)[:, :, 32:64],
                        s_qsb64[:, 0:32].unsqueeze(1).broadcast_to([128, H, 32]), ALU.mult))
                    tt2 = p.v(lambda h: h.tensor_tensor(
                        s_t2b[:, :].rearrange("p (h d) -> p h d", h=H)[:, :, 32:64],
                        s_ub[:, :].rearrange("p (h d) -> p h d", h=H)[:, :, 0:32],
                        s_qsb64[:, 32:64].unsqueeze(1).broadcast_to([128, H, 32]), ALU.mult),
                        inc='eng')
                    tqh = p.v(lambda h: h.tensor_tensor(s_qhb[:, :], s_t1b[:, :], s_t2b[:, :],
                                                        ALU.add),
                              waits=([qb_free] if qb_free else []), inc='eng')
                    ttr = None
                    for k in range(8):
                        ttr = p.t(lambda h, k=k: h.transpose(
                            bptr[:, k * 128:(k + 1) * 128], s_qhb[:, k * 128:(k + 1) * 128],
                            c_idb[:, :]), waits=[tqh] if k == 0 else [],
                            inc='eng' if k == 7 else None)
                    qb_free = ttr
                    tq_toks.append(p.v(lambda h, tb=tb: h.tensor_copy(
                        s_qbuf[:, :].rearrange("p (db f) -> p db f", db=NDB)[:, :, tb * 128:(tb + 1) * 128],
                        bptr[:, :].rearrange("p (db f) -> p db f", db=NDB)), waits=[ttr],
                        inc='eng'))
                if j == nqc - 1 and pi + 1 < len(passes):
                    xn_free = tq_toks[-1]
                    emit_xn(pi + 1)
                mt_toks = []
                tok_dn = None
                for hp in range(HP):
                    def qrhs(qoff, kw, h_, hp=hp):
                        return s_qbuf[64 * h_:64 * h_ + 64, hp * 512 + qoff:hp * 512 + qoff + kw]

                    kb_list = []
                    for kb in range(q0 - 2, q0 + wq_t):
                        qlo = max(kb, q0)
                        qhi = min(kb + 2, q0 + wq_t - 1)
                        if qhi < qlo:
                            continue
                        qoff = (qlo - q0) * 128
                        kw = (qhi - qlo + 1) * 128
                        msk = []
                        for qb_ in range(qlo, qhi + 1):
                            m = maskfor(qb_, kb, E - 16)
                            if m is not None:
                                msk.append(((qb_ - qlo) * 128, m))
                        kb_list.append((kb, qoff, kw, msk))
                    fw = tq_toks + [unit_done, kbuf_free]
                    tok_dn = attn_unit(
                        hp, kb_list, qrhs,
                        kfn=lambda kb, hp=hp: s_kB[:, hp * KT * 128 + (kb - kt0) * 128:
                                                   hp * KT * 128 + (kb - kt0 + 1) * 128],
                        vfn=lambda kb, g: s_vB[:, (kb - kt0) * 1024 + g * 64:
                                               (kb - kt0) * 1024 + (g + 1) * 64],
                        rsqfn=lambda kb, h_, hp=hp: s_rsqB[:, (kb - kt0) * 16 + 2 * hp + h_:
                                                           (kb - kt0) * 16 + 2 * hp + h_ + 1],
                        negc_ap=lambda h_: s_negc[:, :],
                        psA=psA, psB=psB, po0=po0, po1=po1, pd0=pd0, pd1=pd1,
                        first_tok_waits=fw)
                    mt = attn_norm(hp, w, po0, po1, pd0, pd1, psA, tok_dn, fr)
                    mt_toks.append(mt)
                    unit_done = mt
                tst = None
                prev_tst = [None, None]
                for mt_i in range(NDB):
                    ypar = mt_i % 2
                    pwo = po0 if ypar == 0 else po1
                    tyl = p.y(lambda h, mt_i=mt_i, q0=q0, w=w, ypar=ypar: h.dma_start(
                        out=s_yold[:, ypar * 512:ypar * 512 + w],
                        in_=yh[mt_i * 128:(mt_i + 1) * 128,
                               bcol0 + q0 * 128:bcol0 + q0 * 128 + w]),
                        waits=([yold_free] if yold_free else []), inc=DS)
                    tm = None
                    for eb in range(NDB):
                        tm = p.t(lambda h, mt_i=mt_i, eb=eb, w=w, pwo=pwo: h.matmul(
                            pwo[:, 0:w], s_wo6[:, eb * D + mt_i * 128:eb * D + (mt_i + 1) * 128],
                            aostB[:, eb * 512:eb * 512 + w],
                            start=(eb == 0), stop=(eb == NDB - 1)),
                            waits=(([mt_toks[-1]] if mt_i < 2 else [prev_tst[ypar]]))
                                  if eb == 0 else [],
                            inc='eng' if eb == NDB - 1 else None)
                    tst = p.v(lambda h, w=w, pwo=pwo, ypar=ypar: h.tensor_tensor(
                        s_ynew[:, ypar * 512:ypar * 512 + w], pwo[:, 0:w],
                        s_yold[:, ypar * 512:ypar * 512 + w], ALU.add),
                        waits=[tm, tyl] + ([ynew_free[ypar]] if ynew_free[ypar] else []),
                        inc='eng')
                    prev_tst[ypar] = tst
                    yold_free = tst
                    sw = [tst]
                    if pi + 1 < len(passes) and j == nqc - 1:
                        sw += xn_tok[pi + 1]
                    ynew_free[ypar] = p.y(lambda h, mt_i=mt_i, q0=q0, w=w, ypar=ypar: h.dma_start(
                        out=yh[mt_i * 128:(mt_i + 1) * 128,
                               bcol0 + q0 * 128:bcol0 + q0 * 128 + w],
                        in_=s_ynew[:, ypar * 512:ypar * 512 + w]), waits=sw, inc=DS)
                unit_done = tst
            xn_free = tq_toks[-1]
        p.barrier()
        p.flush(block)
        lstk.close()

    s_b1c = bsb("s_b1c", [128, 44], F32)
    s_b2c = bsb("s_b2c", [128, 8], F32)
    s_lng = bsb("s_lng", [128, 8], F32)
    s_lnb = bsb("s_lnb", [128, 8], F32)

    def ffn_sublayer(li, E):
        bcol0 = (BT - E) * 128
        lstk = contextlib.ExitStack()
        lsb = lambda n_, sh_, dt_: lstk.enter_context(nc.sbuf_tensor(f"{n_}f{li}", sh_, dt_))
        s_w2r = lsb("w2r", [128, 22 * D], BF16)
        s_m = lsb("m", [128, 22 * 512], BF16)
        s_x2b = lsb("x2b", [128, NDB * 512], BF16)
        s_xnf = lsb("xnf", [128, NDB * 512], BF16)
        s_w1a = lsb("w1a", [128, 2 * NDB * 128], BF16)
        s_w1g = lsb("w1g", [128, 2 * NDB * 128], BF16)
        s_x2q = lsb("x2q", [128, 512], BF16)
        s_sqf = lsb("sqf", [128, 512], BF16)
        s_ktf = lsb("ktf", [128, 512], BF16)
        pw1a = BANK(0)
        pw1g = BANK(1)
        pff = BANK(2)
        pstat0 = BANK(3)
        pstat1 = BANK(4)
        pbc_r = BANK(5)
        pbc_m = BANK(6)

        pl = []
        for dst, src in [(s_b1c, b1col), (s_b2c, b2col), (s_lng, lngcol), (s_lnb, lnbcol)]:
            pl.append(p.y(lambda h, d=dst, s_=src, li=li: h.dma_start(
                out=d[:, :], in_=s_[li]), inc=DS))
        w2_toks = []
        for ft in range(NFT):
            wA = FW[ft]
            w2_toks.append(p.g(lambda h, ft=ft, wA=wA, li=li: h.dma_start(
                out=s_w2r[0:wA, ft * D:(ft + 1) * D],
                in_=w2[li, ft * 128:ft * 128 + wA, :]), inc=DG))

        nch = (E + 3) // 4
        chs = []
        for ch in range(nch):
            c0 = 4 * ch
            cw_t = min(4, E - c0)
            chs.append((c0, cw_t))
        x2_tok = {}
        x2_free = None

        def emit_x2(ci):
            nonlocal x2_free
            c0, cw_t = chs[ci]
            w = cw_t * 128
            ccol = bcol0 + c0 * 128
            toks = []
            for db in range(NDB):
                sh = 1 if db >= 4 else 0
                toks.append(p.g(lambda h, db=db, ccol=ccol, w=w, sh=sh: h.dma_start(
                    out=s_x2b[:, db * 512:db * 512 + w],
                    in_=yh[db * 128:(db + 1) * 128, ccol - sh:ccol + w - sh]),
                    waits=([x2_free] if (db == 0 and x2_free) else []), inc=DG))
            x2_tok[ci] = toks
            return toks

        emit_x2(0)
        yold_free = [None, None]
        ynew_free = [None, None]
        xn_free2 = None
        m_free = None
        row_free = None
        stat_free = None
        x2q_free = None
        w1_free = [None, None]
        tab_prev = None
        tgb_prev = None
        mtk_prev = None
        tn_prev = None
        for ci, (c0, cw_t) in enumerate(chs):
            w = cw_t * 128
            ccol = bcol0 + c0 * 128
            x2ls = x2_tok[ci]
            tchain = None
            for db in range(NDB):
                tsq = p.s(lambda h, db=db, w=w: h.activation(
                    s_x2q[:, 0:w], s_x2b[:, db * 512:db * 512 + w], AF.Square),
                    waits=[x2ls[db]] + ([x2q_free] if x2q_free else []), inc='eng')
                p.t(lambda h, db=db, w=w: h.matmul(
                    pstat0[0:1, 0:w], c_ocol[:, :], s_x2b[:, db * 512:db * 512 + w],
                    start=(db == 0), stop=(db == NDB - 1)),
                    waits=[x2ls[db]] + ([stat_free] if (db == 0 and stat_free) else []))
                tchain = p.t(lambda h, db=db, w=w: h.matmul(
                    pstat1[0:1, 0:w], c_ocol[:, :], s_x2q[:, 0:w],
                    start=(db == 0), stop=(db == NDB - 1)), waits=[tsq], inc='eng')
                x2q_free = tchain
            wrow = [tchain] + ([row_free] if row_free else [])
            p.v(lambda h, w=w: h.tensor_scalar_mul(s_mu[:, 0:w], pstat0[0:1, 0:w], 1.0 / D),
                waits=wrow)
            p.v(lambda h, w=w: h.tensor_scalar_mul(s_var[:, 0:w], pstat1[0:1, 0:w], 1.0 / D))
            p.v(lambda h, w=w: h.tensor_tensor(s_mr[:, 0:w], s_mu[:, 0:w], s_mu[:, 0:w],
                                               ALU.mult))
            tvr = p.v(lambda h, w=w: h.tensor_tensor(s_var[:, 0:w], s_var[:, 0:w],
                                                     s_mr[:, 0:w], ALU.subtract), inc='eng')
            p.s(lambda h, w=w: h.activation(s_rst[:, 0:w], s_var[:, 0:w], AF.Ln,
                                            bias=s_ecol[0:1, 0:1]), waits=[tvr])
            trs2 = p.s(lambda h, w=w: h.activation(s_rst[:, 0:w], s_rst[:, 0:w], AF.Exp,
                                                   scale=-0.5), inc='eng')
            tmr = p.v(lambda h, w=w: h.tensor_tensor(s_mr[:, 0:w], s_mu[:, 0:w],
                                                     s_rst[:, 0:w], ALU.mult),
                      waits=[trs2], inc='eng')
            bc_toks = []
            for (rowap, pbank, tokw) in ((s_rst, pbc_r, trs2), (s_mr, pbc_m, tmr)):
                wcp = [tokw] + ([row_free] if row_free else []) + \
                      (bc_toks[-1:] if bc_toks else [])
                p.v(lambda h, rowap=rowap, w=w: h.tensor_copy(s_rhi[:, 0:w], rowap[:, 0:w]),
                    waits=wcp)
                p.v(lambda h, rowap=rowap, w=w: h.tensor_tensor(
                    s_rlo32[:, 0:w], rowap[:, 0:w], s_rhi[:, 0:w], ALU.subtract))
                cl = p.v(lambda h, w=w: h.tensor_copy(s_rlo[:, 0:w], s_rlo32[:, 0:w]),
                         inc='eng')
                for hh in range(2):
                    p.t(lambda h, hh=hh, pbank=pbank, w=w: h.matmul(
                        pbank[64 * hh:64 * hh + 64, 0:w], c_orow[:, 0:64], s_rhi[:, 0:w],
                        start=True, stop=False, tile_position=(0, 64 * hh)), waits=[cl])
                    tb2 = p.t(lambda h, hh=hh, pbank=pbank, w=w: h.matmul(
                        pbank[64 * hh:64 * hh + 64, 0:w], c_orow[:, 0:64], s_rlo[:, 0:w],
                        start=False, stop=True, tile_position=(0, 64 * hh)), inc='eng')
                bc_toks.append(tb2)
            row_free = bc_toks[-1]
            txn = None
            for db in range(NDB):
                tn1 = p.v(lambda h, db=db, w=w: h.tensor_tensor(
                    s_t1b[:, 0:w], s_x2b[:, db * 512:db * 512 + w], pbc_r[:, 0:w], ALU.mult),
                    waits=bc_toks + ([tn_prev] if tn_prev else []), inc='eng')
                tn2 = p.v(lambda h, db=db, w=w: h.tensor_tensor(
                    s_t1b[:, 0:w], s_t1b[:, 0:w], pbc_m[:, 0:w], ALU.subtract), inc='eng')
                txn = p.s(lambda h, db=db, w=w: h.activation(
                    s_xnf[:, db * 512:db * 512 + w], s_t1b[:, 0:w], AF.Identity,
                    scale=s_lng[:, db:db + 1], bias=s_lnb[:, db:db + 1]),
                    waits=[tn2, pl[2], pl[3]] + ([xn_free2] if (db == 0 and xn_free2) else []),
                    inc='eng')
                tn_prev = txn
            x2_free = txn
            if ci + 1 < len(chs):
                emit_x2(ci + 1)
            mtk = None
            for ft in range(NFT):
                wA = FW[ft]
                par = ft % 2
                wld = [w1_free[par]] if w1_free[par] else []
                tla = p.g(lambda h, ft=ft, wA=wA, li=li, par=par: h.dma_start(
                    out=s_w1a[:, par * 1024:(par + 1) * 1024].rearrange(
                        "p (db f) -> p db f", db=NDB)[:, :, 0:wA],
                    in_=w1[li, :, ft * 128:ft * 128 + wA].rearrange("(db p) f -> p db f", p=128)),
                    waits=wld, inc=DG)
                tlg = p.g(lambda h, ft=ft, wA=wA, li=li, par=par: h.dma_start(
                    out=s_w1g[:, par * 1024:(par + 1) * 1024].rearrange(
                        "p (db f) -> p db f", db=NDB)[:, :, 0:wA],
                    in_=w1[li, :, DI + ft * 128:DI + ft * 128 + wA].rearrange(
                        "(db p) f -> p db f", p=128)), inc=DG)
                tma = tmg = None
                for db in range(NDB):
                    tma = p.t(lambda h, ft=ft, db=db, wA=wA, w=w, par=par: h.matmul(
                        pw1a[0:wA, 0:w],
                        s_w1a[:, par * 1024 + db * 128:par * 1024 + db * 128 + wA],
                        s_xnf[:, db * 512:db * 512 + w],
                        start=(db == 0), stop=(db == NDB - 1)),
                        waits=([tla, txn] + ([tab_prev] if tab_prev else [])) if db == 0 else [],
                        inc='eng' if db == NDB - 1 else None)
                for db in range(NDB):
                    tmg = p.t(lambda h, ft=ft, db=db, wA=wA, w=w, par=par: h.matmul(
                        pw1g[0:wA, 0:w],
                        s_w1g[:, par * 1024 + db * 128:par * 1024 + db * 128 + wA],
                        s_xnf[:, db * 512:db * 512 + w],
                        start=(db == 0), stop=(db == NDB - 1)),
                        waits=([tlg] + ([tgb_prev] if tgb_prev else [])) if db == 0 else [],
                        inc='eng' if db == NDB - 1 else None)
                tab = p.s(lambda h, ft=ft, wA=wA, w=w: h.activation(
                    s_sqf[0:wA, 0:w], pw1a[0:wA, 0:w], AF.Identity,
                    bias=s_b1c[0:wA, ft:ft + 1]),
                    waits=[tma, pl[0]] + ([mtk_prev] if mtk_prev else []), inc='eng')
                tgb = p.s(lambda h, ft=ft, wA=wA, w=w: h.activation(
                    s_ktf[0:wA, 0:w], pw1g[0:wA, 0:w], AF.Gelu,
                    bias=s_b1c[0:wA, 22 + ft:22 + ft + 1]),
                    waits=[tmg, pl[0]] + ([mtk_prev] if mtk_prev else []), inc='eng')
                tab_prev = tab
                tgb_prev = tgb
                w1_free[par] = tgb
                mtk = p.v(lambda h, ft=ft, wA=wA, w=w: h.tensor_tensor(
                    s_m[0:wA, ft * 512:ft * 512 + w], s_sqf[0:wA, 0:w], s_ktf[0:wA, 0:w],
                    ALU.mult), waits=[tab, tgb] + ([m_free] if (ft == 0 and m_free) else []),
                    inc='eng')
                mtk_prev = mtk
            tst = None
            tm = None
            for mt_i in range(NDB):
                ypar = mt_i % 2
                tyl = p.y(lambda h, mt_i=mt_i, ccol=ccol, w=w, ypar=ypar: h.dma_start(
                    out=s_yold[:, ypar * 512:ypar * 512 + w],
                    in_=yh[mt_i * 128:(mt_i + 1) * 128, ccol:ccol + w]),
                    waits=([yold_free[ypar]] if yold_free[ypar] else []), inc=DS)
                for ft in range(NFT):
                    wA = FW[ft]
                    tm = p.t(lambda h, mt_i=mt_i, ft=ft, wA=wA, w=w: h.matmul(
                        pff[:, 0:w], s_w2r[0:wA, ft * D + mt_i * 128:ft * D + (mt_i + 1) * 128],
                        s_m[0:wA, ft * 512:ft * 512 + w],
                        start=(ft == 0), stop=(ft == NFT - 1)),
                        waits=([mtk] + w2_toks + ([tst] if tst else [])) if ft == 0 else [],
                        inc='eng' if ft == NFT - 1 else None)
                tst = p.v(lambda h, mt_i=mt_i, w=w, ypar=ypar: h.scalar_tensor_tensor(
                    s_ynew[:, ypar * 512:ypar * 512 + w], pff[:, 0:w],
                    s_b2c[:, mt_i:mt_i + 1], s_yold[:, ypar * 512:ypar * 512 + w],
                    ALU.add, ALU.add),
                    waits=[tm, tyl, pl[1]] + ([ynew_free[ypar]] if ynew_free[ypar] else []),
                    inc='eng')
                yold_free[ypar] = tst
                sw = [tst]
                if ci + 1 < len(chs):
                    sw += x2_tok[ci + 1]
                ynew_free[ypar] = p.y(lambda h, mt_i=mt_i, ccol=ccol, w=w, ypar=ypar: h.dma_start(
                    out=yh[mt_i * 128:(mt_i + 1) * 128, ccol:ccol + w],
                    in_=s_ynew[:, ypar * 512:ypar * 512 + w]), waits=sw, inc=DS)
            m_free = tm
            stat_free = tst
        p.barrier()
        p.flush(block)
        lstk.close()

    s_bc8 = bsb("s_bc8", [128, 8], F32)
    s_zc8 = bsb("s_zc8", [128, 8], F32)
    tok_zc = p.y(lambda h: h.dma_start(out=s_zc8[:, :], in_=zc8), inc=DS)

    def fix_boundary_col():
        BCOL = (BT - OUTT) * 128 - 1
        tlc = []
        for db in range(NDB):
            tlc.append(p.y(lambda h, db=db: h.dma_start(
                out=s_bc8[:, db:db + 1],
                in_=yh[db * 128:(db + 1) * 128, BCOL:BCOL + 1],
                allow_slow_non_contiguous=True), inc=DS))
        tv = p.v(lambda h: h.tensor_tensor(s_bc8[:, :], s_bc8[:, :], s_zc8[:, :], ALU.mult),
                 waits=tlc + [tok_zc], inc='eng')
        ts_ = None
        for db in range(NDB):
            ts_ = p.y(lambda h, db=db: h.dma_start(
                out=yh[db * 128:(db + 1) * 128, BCOL:BCOL + 1],
                in_=s_bc8[:, db:db + 1],
                allow_slow_non_contiguous=True), waits=[tv] if db == 0 else [], inc=DS)
        return ts_

    fix_boundary_col()
    p.barrier()
    p.flush(block)

    for li in range(6):
        si_a = 2 * li
        si_f = 2 * li + 1
        if si_a < n_sub:
            attn_sublayer(li, SUB[si_a][0])
            fix_boundary_col()
            p.barrier()
            p.flush(block)
        if si_f < n_sub:
            ffn_sublayer(li, SUB[si_f][0])
            fix_boundary_col()
            p.barrier()
            p.flush(block)

    fstk = contextlib.ExitStack()
    if True:
        fya = fstk.enter_context(nc.sbuf_tensor("fya", [128, 2 * NDB * 128], F32))
        fyb = fstk.enter_context(nc.sbuf_tensor("fyb", [128, 2 * NDB * 128], F32))
        fsum = fstk.enter_context(nc.sbuf_tensor("fsum", [128, 2 * D], F32))
        fout = fstk.enter_context(nc.sbuf_tensor("fout", [128, 2 * D], F32))
        pfo = BANK(0, 2)
        fsum_free = [None, None]
        fout_free = [None, None]
        fy_free = [None, None]
        pfo_free = None
        for ot in range(OUTT):
            bc = (BT - OUTT + ot) * 128
            par = ot % 2
            tlf = []
            for db in range(NDB):
                tlf.append(p.y(lambda h, db=db, bc=bc, par=par: h.dma_start(
                    out=fya[:, (par * NDB + db) * 128:(par * NDB + db + 1) * 128],
                    in_=xah[db * 128:(db + 1) * 128, bc:bc + 128]),
                    waits=([fy_free[par]] if (db == 0 and fy_free[par]) else []), inc=DS))
                tlf.append(p.y(lambda h, db=db, bc=bc, par=par: h.dma_start(
                    out=fyb[:, (par * NDB + db) * 128:(par * NDB + db + 1) * 128],
                    in_=yh[db * 128:(db + 1) * 128, bc:bc + 128]), inc=DS))
            ta = None
            for db in range(NDB):
                if dbg_out == 'xa':
                    ta = p.v(lambda h, db=db, par=par: h.tensor_copy(
                        fsum[:, par * D + db * 128:par * D + (db + 1) * 128],
                        fya[:, (par * NDB + db) * 128:(par * NDB + db + 1) * 128]),
                        waits=(tlf if db == 0 else []) +
                              ([fsum_free[par]] if (db == 0 and fsum_free[par]) else []),
                        inc='eng' if db == NDB - 1 else None)
                elif dbg_out == 'y':
                    ta = p.v(lambda h, db=db, par=par: h.tensor_copy(
                        fsum[:, par * D + db * 128:par * D + (db + 1) * 128],
                        fyb[:, (par * NDB + db) * 128:(par * NDB + db + 1) * 128]),
                        waits=(tlf if db == 0 else []) +
                              ([fsum_free[par]] if (db == 0 and fsum_free[par]) else []),
                        inc='eng' if db == NDB - 1 else None)
                else:
                    ta = p.v(lambda h, db=db, par=par: h.tensor_tensor(
                        fsum[:, par * D + db * 128:par * D + (db + 1) * 128],
                        fya[:, (par * NDB + db) * 128:(par * NDB + db + 1) * 128],
                        fyb[:, (par * NDB + db) * 128:(par * NDB + db + 1) * 128],
                        ALU.add),
                        waits=(tlf if db == 0 else []) +
                              ([fsum_free[par]] if (db == 0 and fsum_free[par]) else []),
                        inc='eng' if db == NDB - 1 else None)
            fy_free[par] = ta
            ttr = None
            for k in range(8):
                ttr = p.t(lambda h, k=k, par=par: h.transpose(
                    pfo[:, k * 128:(k + 1) * 128], fsum[:, par * D + k * 128:par * D + (k + 1) * 128],
                    c_id32[:, :]),
                    waits=([ta] + ([pfo_free] if pfo_free else [])) if k == 0 else [],
                    inc='eng' if k == 7 else None)
            fsum_free[par] = ttr
            tev = p.s(lambda h, par=par: h.copy(fout[:, par * D:(par + 1) * D], pfo[:, :]),
                      waits=[ttr] + ([fout_free[par]] if fout_free[par] else []), inc='eng')
            pfo_free = tev
            fout_free[par] = p.y(lambda h, ot=ot, par=par: h.dma_start(
                out=out[ot * 128:(ot + 1) * 128, :], in_=fout[:, par * D:(par + 1) * D]),
                waits=[tev], inc=DS)
    for ln in range(Prog.NLANE):
        key = f"{DS}{ln}"
        final_tok = p.cnt[key]
        if final_tok > p.water['sync'][key]:
            p.q['sync'].append(('WAIT', (key, final_tok), None, ''))
            p.water['sync'][key] = final_tok
    p.flush(block)
    fstk.close()
    bstk.close()
    stack.close()
    return nc, p


def _host_prep(inputs):
    import ml_dtypes
    bf = ml_dtypes.bfloat16
    f32 = lambda a: np.ascontiguousarray(np.asarray(a, dtype=np.float32))

    X = f32(inputs["L0_attn_x"])
    shared = {
        "wq": f32(inputs["L0_attn_Wq"]), "wk": f32(inputs["L0_attn_Wk"]),
        "wv": f32(inputs["L0_attn_Wv"]), "wo": f32(inputs["L0_attn_Wo"]),
        "wqkv": f32(inputs["L0_ffn_Wqkv"]), "wo6": f32(inputs["L0_ffn_Wo"]),
        "w1": f32(inputs["L0_ffn_W1"]), "w2": f32(inputs["L0_ffn_W2"]),
    }
    bo = f32(inputs["L0_attn_bo"]).reshape(8, 128)
    shared["bocol"] = np.ascontiguousarray(bo.T)
    qsa = f32(inputs["L0_attn_q_scale"]).reshape(D)
    ksa = f32(inputs["L0_attn_k_scale"]).reshape(D)
    shared["qsma"] = np.ascontiguousarray(np.broadcast_to(qsa, (128, D)))
    shared["ksma"] = np.ascontiguousarray(np.broadcast_to(ksa, (128, D)))
    shared["kscola"] = np.ascontiguousarray(ksa.reshape(8, 128).T)
    b1 = f32(inputs["L0_ffn_b1"])
    b1c = np.zeros((6, 128, 44), np.float32)
    for ft in range(NFT):
        wA = FW[ft]
        b1c[:, 0:wA, ft] = b1[:, ft * 128:ft * 128 + wA]
        b1c[:, 0:wA, 22 + ft] = b1[:, DI + ft * 128:DI + ft * 128 + wA]
    shared["b1col"] = b1c
    for name, key in [("b2col", "L0_ffn_b2"), ("lngcol", "L0_ffn_ln_g"),
                      ("lnbcol", "L0_ffn_ln_b")]:
        shared[name] = np.ascontiguousarray(
            f32(inputs[key]).reshape(6, 8, 128).transpose(0, 2, 1))
    qs6 = f32(inputs["L0_ffn_q_scale"])
    ks6 = f32(inputs["L0_ffn_k_scale"])
    shared["qsm6"] = np.ascontiguousarray(np.broadcast_to(qs6[:, None, :], (6, 128, DH)))
    qs6s = np.concatenate([qs6[:, 32:], qs6[:, :32]], axis=1)
    shared["qsm6s"] = np.ascontiguousarray(np.broadcast_to(qs6s[:, None, :], (6, 128, DH)))
    shared["ksm6"] = np.ascontiguousarray(np.broadcast_to(ks6[:, None, :], (6, 128, DH)))
    shared["kscol6"] = np.ascontiguousarray(
        np.concatenate([ks6, ks6], axis=1).reshape(6, 128, 1))
    shared["ecol"] = np.ascontiguousarray(np.broadcast_to(
        np.array([1e-5, 1e-20, 1e-18, 1e-22], np.float32), (128, 4)))
    shared["ident32"] = np.eye(128, dtype=np.float32)
    shared["identb"] = np.eye(128, dtype=np.float32).astype(bf)
    shared["pib"] = _swap_perm()
    mdiag, mfar = _const_masks()
    shared["mdiag"] = mdiag
    shared["mfar"] = mfar
    shared["onescol"] = np.ones((128, 1), np.float32).astype(bf)
    oblk = np.zeros((128, 2), np.float32)
    oblk[0:64, 0] = 1.0
    oblk[64:128, 1] = 1.0
    shared["onesblk"] = oblk.astype(bf)
    shared["onesrow"] = np.ones((1, 128), np.float32).astype(bf)

    var = {}
    for left in (True, False):
        toks = np.arange(BT * 128) - (2048 if left else 0)
        ktabA, ktabB0 = None, None
        ktA, qt = _tables_for_core(toks)
        Ak = ktA[0:64]
        Bk = ktA[64:128]
        var[left] = {
            "ktabA": np.ascontiguousarray(np.concatenate([Ak, Ak], axis=0)),
            "ktabB": np.ascontiguousarray(np.concatenate([Bk, Bk], axis=0)),
            "qtab": qt,
            "pmask": _masks_for_core(left),
            "zc8": np.full((128, 8), 0.0 if left else 1.0, np.float32),
        }

    in_maps = []
    for c in range(NCORES):
        row, left = c // 2, (c % 2 == 0)
        x = X[row]
        if left:
            xqv = np.zeros((EA * 128, D), np.float32)
            xqv[1664:] = x[0:2048]
        else:
            xqv = np.ascontiguousarray(x[384:4096])
        m = dict(shared)
        m.update(var[left])
        m["xkv"] = x
        m["xq"] = xqv
        in_maps.append(m)
    return in_maps


_N_SUB = 12
_NC_CACHE = {}


_DBG_OUT = None


_DUMPS = False


def kernel(**inputs):
    n_sub = _N_SUB
    key = (n_sub, _DBG_OUT, _DUMPS)
    if key not in _NC_CACHE:
        _NC_CACHE[key] = build_nc(n_sub, _DBG_OUT, _DUMPS)
    nc, _p = _NC_CACHE[key]
    in_maps = _host_prep(inputs)
    res = run_bass_kernel_spmd(nc, in_maps, core_ids=list(range(NCORES)))
    if _DUMPS:
        globals()['_LAST_RES'] = res
    rows = []
    for r in range(4):
        a = np.asarray(res.results[2 * r]["out"])
        b = np.asarray(res.results[2 * r + 1]["out"])
        rows.append(np.concatenate([a, b], axis=0))
    return np.stack(rows, axis=0).astype(np.float32)
